# version 13 (final)
# speedup vs baseline: 1.5066x; 1.0099x over previous
import math

import numpy as np
import ml_dtypes

import concourse.bass as bass
import concourse.mybir as mybir
from concourse.bass_utils import run_bass_kernel_spmd

F32 = mybir.dt.float32
BF16 = mybir.dt.bfloat16
AF = mybir.ActivationFunctionType
ALU = mybir.AluOpType
AX = mybir.AxisListType

NT = 2048
EMB = 1024
NH = 16
TT = 16
NEXP = 8
MI = 1408
ICH = 11
SI = 2816
RBW = 4608
RWINW = 3972
VGS = 66
LNEPS = 1e-5
DEBUG = False
UPTO = 99
APART = 9
MPART = 9


class Prog:
    ENGINES = ("tensor", "vector", "scalar", "sync", "gpsimd")

    def __init__(self):
        self.ops = {e: [] for e in self.ENGINES}
        self.semn = {}

    def emit(self, eng, fn=None, waits=(), inc=None, drain=False):
        v = None
        incs = []
        if inc is not None:
            incs = [inc] if isinstance(inc, tuple) else list(inc)
            for s, k in incs:
                self.semn[s] = self.semn.get(s, 0) + k
            v = self.semn[incs[0][0]]
        self.ops[eng].append(
            (fn, [(s, w) for (s, w) in waits if w > 0], incs, drain))
        return v

    def val(self, s):
        return self.semn.get(s, 0)

    def barrier(self):
        vals = {}
        for e in self.ENGINES:
            vals[e] = self.emit(e, lambda en: en.nop(), inc=("bar_" + e, 1))
        for e in self.ENGINES:
            w = [("bar_" + o, vals[o]) for o in self.ENGINES if o != e]
            self.ops[e].append((None, w, [], False))


def _bucket_np(rel):
    nb = 160
    ret = (rel > 0).astype(np.int32) * nb
    ra = np.abs(rel)
    is_small = ra < 80
    large = (
        np.log(np.maximum(ra, 1).astype(np.float32) / 80.0)
        / math.log(800.0 / 80.0) * 80.0
    )
    large = (80 + large).astype(np.int32)
    large = np.minimum(large, 159)
    return ret + np.where(is_small, ra.astype(np.int32), large)


def _host_onehot():
    oh = np.zeros((320, RBW), np.float32)
    j = np.arange(1, 4096)
    b = _bucket_np(j - 2048)
    oh[b, j] = 1.0
    return oh


def build_nc(upto=None):
    if upto is None:
        upto = UPTO
    nc = bass.Bass("TRN2", target_bir_lowering=False,
                   detect_race_conditions=False)

    d_x = nc.dram_tensor("x", [NT, EMB], F32, kind="ExternalInput")
    d_wq = nc.dram_tensor("wqT", [EMB, EMB], F32, kind="ExternalInput")
    d_wk = nc.dram_tensor("wkT", [EMB, EMB], F32, kind="ExternalInput")
    d_wv = nc.dram_tensor("wvT", [EMB, EMB], F32, kind="ExternalInput")
    d_wo = nc.dram_tensor("woT", [EMB, EMB], F32, kind="ExternalInput")
    d_bcat = nc.dram_tensor("bcat", [1, 4352], F32, kind="ExternalInput")
    d_gwb = nc.dram_tensor("gruWblk", [EMB, 128], F32, kind="ExternalInput")
    d_gs = nc.dram_tensor("gruS", [128, 32], F32, kind="ExternalInput")
    d_gc = nc.dram_tensor("gruconst", [16, 1], F32, kind="ExternalInput")
    d_id = nc.dram_tensor("ident", [128, 128], F32, kind="ExternalInput")
    d_re = nc.dram_tensor("relemb", [320, 16], F32, kind="ExternalInput")
    d_oh = nc.dram_tensor("oh", [320, RBW], F32, kind="ExternalInput")
    d_gw9 = nc.dram_tensor("gateWseg", [EMB, 9], F32, kind="ExternalInput")
    d_egu = nc.dram_tensor("eguT", [NEXP, EMB, 2 * MI], F32, kind="ExternalInput")
    d_ed = nc.dram_tensor("edT", [NEXP, MI, EMB], F32, kind="ExternalInput")
    d_sg = nc.dram_tensor("sgT", [EMB, SI], F32, kind="ExternalInput")
    d_su = nc.dram_tensor("suT", [EMB, SI], F32, kind="ExternalInput")
    d_sd = nc.dram_tensor("sdT", [SI, EMB], F32, kind="ExternalInput")
    d_out = nc.dram_tensor("out", [NT, EMB], F32, kind="ExternalOutput")
    if DEBUG:
        d_dbg_w = nc.dram_tensor("dbg_w", [128, 128], F32, kind="ExternalOutput")
        d_dbg_sw = nc.dram_tensor("dbg_sw", [128, 16], F32, kind="ExternalOutput")
        d_dbg_x2 = nc.dram_tensor("dbg_x2", [128, 8 * NT], BF16, kind="ExternalOutput")
        d_dbg_acc = nc.dram_tensor("dbg_acc", [128, 8192], F32, kind="ExternalOutput")
        d_dbg_h = nc.dram_tensor("dbg_h", [128, ICH * 1024], BF16, kind="ExternalOutput")
        d_dbg_ao = nc.dram_tensor("dbg_ao", [128, 8 * NT], BF16, kind="ExternalOutput")
        d_dbg_mb = nc.dram_tensor("dbg_mb", [128, 16], F32, kind="ExternalOutput")
        d_dbg_g8 = nc.dram_tensor("dbg_g8", [16, NT], F32, kind="ExternalOutput")
    d_rbd = nc.dram_tensor("rbd", [16, RBW], BF16)
    d_x1 = nc.dram_tensor("x1d", [NT, EMB], F32,
                          kind="ExternalOutput" if DEBUG else "Internal")

    BASE = (int(nc.dma_scratch.shape[1]) if nc.dma_scratch is not None
            else 16384) + 1024
    LIMIT = 229376 - 64 - BASE

    def at(name, shape, dtype, off):
        sz = int(np.prod(shape[1:])) * mybir.dt.size(dtype)
        assert off + sz <= LIMIT, (name, off + sz)
        return nc.alloc_sbuf_tensor_at(name, shape, dtype, offset=off + BASE)

    class Bump:
        def __init__(self, base, limit):
            self.o, self.limit = base, limit

        def __call__(self, name, shape, dtype):
            t = at(name, shape, dtype, self.o)
            sz = int(np.prod(shape[1:])) * mybir.dt.size(dtype)
            self.o += (sz + 31) & ~31
            assert self.o <= self.limit, (name, self.o)
            return t

    per = Bump(0, LIMIT)
    ones_bf = per("ones_bf", [128, 512], BF16)
    ones_f32 = per("ones_f32", [128, 128], F32)
    stats = per("stats", [128, 16], F32)
    mbcast = per("mbcast", [128, 16], F32)
    wtok = per("wtok", [128, 128], F32)
    swtok = per("swtok", [128, 16], F32)
    rcalc = per("rcalc", [128, 32], F32)
    gconst = per("gconst", [16, 1], F32)
    bnd = per("bnd", [16, 16], F32)
    negmrow = per("negmrow", [1, 16], F32)
    biases = per("biases", [1, 4352], BF16)
    gruS = per("gruSb", [128, 32], BF16)
    ident = per("identb", [128, 128], BF16)
    gw9 = per("gw9b", [128, 72], BF16)
    maskblk = per("maskblk", [128, 128], BF16)
    lnsq = per("lnsq", [128, 1024], BF16)
    constf = per("constf", [128, 8], F32)
    CONSTS = (0.0, 1.0, -1.0, 0.125, LNEPS)
    for ci_, cv_ in enumerate(CONSTS):
        nc.const_aps.register(F32, cv_, constf[:, ci_:ci_ + 1])
    PEND = per.o

    B_QT = PEND
    B_KT = B_QT + 32768
    B_AO = B_KT + 32768
    B_VA = B_AO + 32768
    B_G8 = B_VA + 33792
    B_SC = B_G8 + 4096 + 2 * 4096

    qT = at("qT", [128, 8 * NT], BF16, B_QT)
    x2lnT = at("x2lnT", [128, 8 * NT], BF16, B_QT)
    kT = at("kT", [128, 8 * NT], BF16, B_KT)
    attnoutT = at("attnoutT", [128, 8 * NT], BF16, B_AO)
    xlnT = at("xlnT", [128, 8 * NT], BF16, B_AO)
    v_aug = at("v_aug", [128, 16 * NH * VGS], BF16, B_VA)
    g8 = at("g8", [16, NT], BF16, B_G8)
    gcur = [at(f"gcur{i}", [1, NT], BF16, B_G8 + 4096 + i * 4096) for i in (0, 1)]

    ohsb = [at("oh0", [128, RBW], F32, B_QT),
            at("oh1", [128, RBW], F32, B_QT + RBW * 4),
            at("oh2", [64, RBW], F32, B_QT + 2 * RBW * 4)]
    resb = [at("re0", [128, 16], F32, B_QT + 3 * RBW * 4),
            at("re1", [128, 16], F32, B_QT + 3 * RBW * 4 + 64),
            at("re2", [64, 16], F32, B_QT + 3 * RBW * 4 + 128)]

    pr = Bump(B_SC, LIMIT)
    xtok = [pr(f"xtok{i}", [128, EMB], F32) for i in (0, 1)]
    xln = [pr(f"xln{i}", [128, EMB], BF16) for i in (0, 1)]
    wstream = [pr(f"wstream{i}", [128, 1024], BF16) for i in (0, 1)]
    wvhalf = pr("wvhalf", [128, 8 * 512], BF16)
    projb = at("projb", [128, NT], BF16, wvhalf.manual_sbuf_range[0] - BASE)
    gab = pr("gab", [16, NT], BF16)
    gbb = pr("gbb", [16, NT], BF16)
    qn = at("qn", [16, NT], F32, gab.manual_sbuf_range[0] - BASE)
    sqt = pr("sqt", [128, NT], BF16)
    rbsb = pr("rbsb", [16, RBW], BF16)

    asx = Bump(B_SC, LIMIT)
    rwin = [asx(f"rwin{i}", [128, RWINW], BF16) for i in (0, 1)]
    gb8c = [asx(f"gb8c{i}", [128, NT], BF16) for i in (0, 1)]
    tmpb = [asx(f"tmpb{i}", [128, 1024], BF16) for i in (0, 1, 2)]
    attT = [asx(f"attT{i}", [128, 1024], BF16) for i in (0, 1, 2)]
    rec = asx("rec", [128, 1024], F32)
    recall = asx("recall", [16, NT], F32)
    stage = asx("stage", [128, NT], BF16)

    cc = Bump(B_KT, B_AO)
    af = [cc(f"af{i}", [128, EMB], F32) for i in (0, 1)]
    x1t = [cc(f"x1t{i}", [128, EMB], F32) for i in (0, 1)]
    xlnC = [cc(f"xlnC{i}", [128, EMB], BF16) for i in (0, 1)]
    xtokC = [cc(f"xtokC{i}", [128, EMB], F32) for i in (0, 1)]
    woSt = [at(f"woSt{i}", [128, 4096], BF16, B_VA + i * 8192) for i in (0, 1)]
    rrow = at("rrow", [1, NT], F32, B_VA + 16384)

    dd = Bump(B_KT + 28672, LIMIT)
    hbuf = dd("hbuf", [128, ICH * 1024], BF16)
    wd = [dd(f"wd{i}", [128, ICH * 1024], BF16) for i in (0, 1)]
    wg = [dd(f"wg{i}", [128, 1024], BF16) for i in (0, 1)]
    wu = [dd(f"wu{i}", [128, 1024], BF16) for i in (0, 1)]
    acc = dd("acc", [128, 8 * 1024], F32)
    gsil = [dd(f"gsil{i}", [128, 512], BF16) for i in (0, 1)]
    outt = [dd(f"outt{i}", [128, EMB], F32) for i in (0, 1)]
    x1r = [dd(f"x1r{i}", [128, EMB], F32) for i in (0, 1)]

    PS = [nc.alloc_psum_tensor(f"ps{i}", [128, 1024], F32) for i in range(4)]
    pb = [PS[i // 2][:, (i % 2) * 512:(i % 2) * 512 + 512] for i in range(8)]
    psQK = (PS[0], PS[1])
    psPV = (PS[2], PS[3])

    P = Prog()

    def sap(t, off, dims):
        if isinstance(t, bass.AP):
            return bass.AP(t.tensor, t.offset + off, dims)
        return bass.AP(t, off, dims)

    def ln_stats(src_fn, mcol, rcol, wait_v=(), wait_s=()):
        P.emit("vector", lambda e: e.reduce_sum(
            stats[:, mcol:mcol + 1], src_fn(e), AX.X), waits=wait_v)
        P.emit("scalar", lambda e: e.activation(
            lnsq[:, :], src_fn(e), AF.Square,
            accum_out=stats[:, rcol:rcol + 1]),
            waits=list(wait_s) + [("s_lnu", P.val("s_lnu"))],
            inc=("s_lsq", 1))
        lsq = P.val("s_lsq")
        P.emit("vector", lambda e: e.tensor_scalar_mul(
            stats[:, mcol:mcol + 1], stats[:, mcol:mcol + 1], 1.0 / EMB),
            drain=True)
        P.emit("vector", lambda e: e.tensor_scalar_mul(
            stats[:, rcol:rcol + 1], stats[:, rcol:rcol + 1], 1.0 / EMB),
            waits=[("s_lsq", lsq)])
        P.emit("vector", lambda e: e.scalar_tensor_tensor(
            stats[:, rcol:rcol + 1], stats[:, mcol:mcol + 1],
            stats[:, mcol:mcol + 1], stats[:, rcol:rcol + 1],
            ALU.mult, ALU.subtract), inc=("s_lnv", 1), drain=True)
        P.emit("scalar", lambda e: e.activation(
            stats[:, rcol:rcol + 1], stats[:, rcol:rcol + 1], AF.Sqrt,
            bias=LNEPS, scale=-1.0),
            waits=[("s_lnv", P.val("s_lnv"))], inc=("s_lsd", 1))
        P.emit("vector", lambda e: e.reciprocal(
            stats[:, rcol:rcol + 1], stats[:, rcol:rcol + 1]),
            waits=[("s_lsd", P.val("s_lsd"))], inc=("s_lnst", 1))
        return P.val("s_lnst")

    def phase0():
        P.emit("vector", lambda e: e.memset(ones_bf[:], 1.0))
        P.emit("vector", lambda e: e.memset(ones_f32[:], 1.0))
        P.emit("vector", lambda e: e.memset(maskblk[:], 0.0))
        for ci_, cv_ in enumerate(CONSTS):
            P.emit("vector", lambda e, ci_=ci_, cv_=cv_: e.memset(
                constf[:, ci_:ci_ + 1], cv_))
        P.emit("vector", lambda e: e.memset(
            sap(v_aug, 64, [[16 * NH * VGS, 128], [VGS, 16 * NH], [1, 1]]), 1.0))
        for j in range(8):
            P.emit("vector", lambda e, j=j: e.memset(
                maskblk[0:64, 16 * j + 2 * j: 16 * j + 2 * j + 1], 1.0),
                drain=(j == 0))
            P.emit("vector", lambda e, j=j: e.memset(
                maskblk[64:128, 16 * j + 2 * j + 1: 16 * j + 2 * j + 2], 1.0))
        P.emit("vector", lambda e: e.engine_nop(), inc=("s_const", 1))
        P.emit("scalar", None, waits=[("s_const", 1)])
        P.emit("tensor", None, waits=[("s_const", 1)])

        for dst, src in ((biases, d_bcat), (gruS, d_gs), (ident, d_id)):
            P.emit("gpsimd", lambda e, a=dst, b=src: e.dma_start(a[:, :], b[:, :]),
                   inc=("s_small", 16))
        P.emit("gpsimd", lambda e: e.dma_start(
            sap(gw9, 0, [[72, 128], [9, 8], [1, 9]]),
            sap(d_gw9, 0, [[9, 128], [128 * 9, 8], [1, 9]])),
            inc=("s_small", 16))
        P.emit("sync", lambda e: e.dma_start(gconst[:, :], d_gc[:, :]),
               inc=("s_smalls", 16))
        P.emit("sync", lambda e: e.dma_start(ohsb[0][:, :], d_oh[0:128, :]),
               inc=("s_oh", 16))
        P.emit("sync", lambda e: e.dma_start(ohsb[1][:, :], d_oh[128:256, :]),
               inc=("s_oh", 16))
        P.emit("sync", lambda e: e.dma_start(ohsb[2][:, :], d_oh[256:320, :]),
               inc=("s_oh", 16))
        for i, (lo, hi) in enumerate(((0, 128), (128, 256), (256, 320))):
            P.emit("sync", lambda e, i=i, lo=lo, hi=hi: e.dma_start(
                resb[i][:, :], d_re[lo:hi, :]), inc=("s_oh", 16))
        ohv = P.val("s_oh")

        for c in range(RBW // 512):
            bank = pb[c % 2]
            for ci in range(3):
                w = []
                if ci == 0:
                    w = [("s_oh", ohv), ("s_rbe", max(0, P.val("s_rbe") - 1))]
                P.emit("tensor", lambda e, c=c, ci=ci, bank=bank: e.matmul(
                    bank[0:16, :], resb[ci][:, :],
                    ohsb[ci][:, c * 512:(c + 1) * 512],
                    start=(ci == 0), stop=(ci == 2)), waits=w,
                    inc=("s_rbm", 1) if ci == 2 else None)
            P.emit("vector", lambda e, c=c, bank=bank: e.tensor_copy(
                rbsb[:, c * 512:(c + 1) * 512], bank[0:16, :]),
                waits=[("s_rbm", P.val("s_rbm"))], inc=("s_rbe", 1))
        P.emit("sync", lambda e: e.dma_start(d_rbd[:, :], rbsb[:, :]),
               waits=[("s_rbe", P.val("s_rbe"))], inc=("s_rbd", 16))
        P.emit("vector", lambda e: e.tensor_reduce(
            bnd[:, 0:1], rbsb[:, :], AX.X, ALU.max, apply_absolute_value=True),
            inc=("s_bnd", 1))

    def phaseA():
        smallv = P.val("s_small")
        xuse = []
        for t in range(TT):
            b = t % 2
            w = [("s_xt", xuse[t - 2])] if t >= 2 else []
            P.emit("gpsimd", lambda e, t=t, b=b: e.dma_start(
                xtok[b][:, :], d_x[t * 128:(t + 1) * 128, :]),
                waits=w, inc=("s_xin", 16))
            xv = P.val("s_xin")
            sv = ln_stats(lambda e, b=b: xtok[b][:, :], 0, 1,
                          wait_v=[("s_xin", xv)], wait_s=[("s_xin", xv)])
            wN = [("s_lnst", sv)]
            if t >= 2:
                wN.append(("s_xt", xuse[t - 2]))
            P.emit("vector", lambda e, b=b: e.tensor_scalar(
                xln[b][:, :], xtok[b][:, :], stats[:, 0:1], stats[:, 1:2],
                ALU.subtract, ALU.mult), waits=wN,
                inc=[("s_xln", 1), ("s_lnu", 1)])
            xlv = P.val("s_xln")
            for c in range(8):
                P.emit("sync", lambda e, t=t, b=b, c=c: e.dma_start_transpose(
                    xlnT[:, c * NT + t * 128: c * NT + (t + 1) * 128],
                    xln[b][:, c * 128:(c + 1) * 128]),
                    waits=[("s_xln", xlv)] if c == 0 else [],
                    inc=("s_xt", 16))
            xuse.append(P.val("s_xt"))
        xlnT_ready = P.val("s_xt")
        xt_chunk = lambda tc: min(xlnT_ready, 128 * (4 * tc + 4))

        def proj_fm(dst, d_w, bias_off, sem_mm, evpar, prev_gate, first_waits):
            consumed = []
            for j in range(8):
                wb = j % 2
                w = []
                if j >= 2:
                    w.append((sem_mm, consumed[j - 2]))
                elif prev_gate is not None:
                    w.append(prev_gate)
                P.emit("gpsimd", lambda e, j=j, wb=wb: e.dma_start(
                    sap(wstream[wb], 0, [[1024, 128], [128, 8], [1, 128]]),
                    sap(d_w, j * 128, [[EMB, 128], [128 * EMB, 8], [1, 128]])),
                    waits=w, inc=("s_ws" + str(wb), 16))
                wv = P.val("s_ws" + str(wb))
                for tc in range(4):
                    par = (j * 4 + tc) % 4
                    bank = pb[par]
                    evw = P.val(evpar + str(par))
                    for ec in range(8):
                        w = []
                        if ec == 0:
                            w = [("s_ws" + str(wb), wv),
                                 ("s_xt", xt_chunk(tc)),
                                 (evpar + str(par), evw)]
                            if j == 0 and tc == 0:
                                w += first_waits
                        P.emit("tensor", lambda e, j=j, tc=tc, ec=ec, wb=wb,
                               bank=bank: e.matmul(
                                   bank[:, :],
                                   wstream[wb][:, ec * 128:(ec + 1) * 128],
                                   xlnT[:, ec * NT + tc * 512: ec * NT + tc * 512 + 512],
                                   start=(ec == 0), stop=False), waits=w)
                    P.emit("tensor", lambda e, j=j, bank=bank, bo=bias_off:
                           e.matmul(bank[:, :],
                                    biases[0:1, bo + j * 128: bo + (j + 1) * 128],
                                    ones_bf[0:1, 0:512], start=False, stop=True),
                           inc=(sem_mm, 1))
                    mmv = P.val(sem_mm)
                    P.emit("vector", lambda e, j=j, tc=tc, bank=bank, dst=dst:
                           e.tensor_copy(
                               dst[:, j * NT + tc * 512: j * NT + tc * 512 + 512],
                               bank[:, :]),
                           waits=[(sem_mm, mmv)], inc=(evpar + str(par), 1))
                consumed.append(P.val(sem_mm))

        if APART < 2:
            return
        proj_fm(qT, d_wq, 0, "s_qmm", "s_qe",
                None, [("s_small", smallv), ("s_smalls", P.val("s_smalls")),
                       ("s_rbm", P.val("s_rbm")),
                       ("s_rbe", P.val("s_rbe")), ("s_const", 1)])
        if APART < 3:
            return
        proj_fm(kT, d_wk, 1024, "s_kmm", "s_ke",
                ("s_qmm", P.val("s_qmm")), [])
        if APART < 4:
            return

        for dvc in range(2):
            w = [("s_vmm", P.val("s_vmm"))] if dvc else \
                [("s_kmm", P.val("s_kmm"))]
            P.emit("gpsimd", lambda e, dvc=dvc: e.dma_start(
                sap(wvhalf, 0, [[4096, 128], [512, 8], [1, 512]]),
                sap(d_wv, dvc * 512, [[EMB, 128], [128 * EMB, 8], [1, 512]])),
                waits=w, inc=("s_wv", 16))
            wv = P.val("s_wv")
            for t in range(TT):
                par = t % 2
                bank = pb[4 + par]
                evw = P.val("s_ve" + str(par))
                for ec in range(8):
                    w = []
                    if ec == 0:
                        w = [("s_wv", wv), ("s_xt", min(xlnT_ready, 128 * (t + 1))),
                             ("s_ve" + str(par), evw)]
                    P.emit("tensor", lambda e, t=t, ec=ec, bank=bank: e.matmul(
                        bank[:, :],
                        xlnT[:, ec * NT + t * 128: ec * NT + t * 128 + 128],
                        wvhalf[:, ec * 512:(ec + 1) * 512],
                        start=(ec == 0), stop=False), waits=w)
                P.emit("tensor", lambda e, dvc=dvc, bank=bank: e.matmul(
                    bank[:, :], ones_bf[0:1, 0:128],
                    biases[0:1, 2048 + dvc * 512: 2048 + dvc * 512 + 512],
                    start=False, stop=True), inc=("s_vmm", 1))
                mmv = P.val("s_vmm")
                off = (t * NH + dvc * 8) * VGS
                P.emit("vector", lambda e, off=off, bank=bank: e.tensor_copy(
                    sap(v_aug, off, [[16 * NH * VGS, 128], [VGS, 8], [1, 64]]),
                    sap(bank, 0, [[1024, 128], [64, 8], [1, 64]])),
                    waits=[("s_vmm", mmv)], inc=("s_ve" + str(par), 1))

        if APART < 5:
            return
        P.emit("gpsimd", lambda e: e.dma_start(
            sap(wstream[0], 0, [[1024, 128], [128, 8], [1, 128]]),
            sap(d_gwb, 0, [[128, 128], [128 * 128, 8], [1, 128]])),
            waits=[("s_kmm", P.val("s_kmm"))], inc=("s_ws0", 16))
        gwv = P.val("s_ws0")
        for tc in range(4):
            bank = pb[6 + tc % 2]
            for ec in range(8):
                w = []
                if ec == 0:
                    w = [("s_ws0", gwv), ("s_xt", xt_chunk(tc))]
                    if tc >= 2:
                        w.append(("s_gpe", P.val("s_gpe") - 1))
                P.emit("tensor", lambda e, tc=tc, ec=ec, bank=bank: e.matmul(
                    bank[:, :], wstream[0][:, ec * 128:(ec + 1) * 128],
                    xlnT[:, ec * NT + tc * 512: ec * NT + tc * 512 + 512],
                    start=(ec == 0), stop=False), waits=w)
            P.emit("tensor", lambda e, bank=bank: e.matmul(
                bank[:, :], biases[0:1, 4096:4224], ones_bf[0:1, 0:512],
                start=False, stop=True), inc=("s_gpm", 1))
            P.emit("vector", lambda e, tc=tc, bank=bank: e.tensor_copy(
                projb[:, tc * 512:(tc + 1) * 512], bank[:, :]),
                waits=[("s_gpm", P.val("s_gpm"))], inc=("s_gpe", 1))
        gpe = P.val("s_gpe")
        if APART < 6:
            return
        for tc in range(4):
            bka, bkb = pb[6], pb[7]
            P.emit("tensor", lambda e, tc=tc, bka=bka: e.matmul(
                bka[0:16, :], gruS[:, 0:16], projb[:, tc * 512:(tc + 1) * 512],
                start=True, stop=True),
                waits=([("s_gpe", gpe)] if APART < 7 and APART >= 6.5 else
                       [("s_gpe", gpe), ("s_gse", P.val("s_gse"))]),
                inc=("s_gsm", 1))
            P.emit("tensor", lambda e, tc=tc, bkb=bkb: e.matmul(
                bkb[0:16, :], gruS[:, 16:32], projb[:, tc * 512:(tc + 1) * 512],
                start=True, stop=True), inc=("s_gsm", 1))
            gsm = P.val("s_gsm")
            if APART >= 6.5 and APART < 7:
                continue
            if APART == 6.7:
                P.emit("vector", lambda e, tc=tc, bka=bka: e.tensor_copy(
                    gab[:, tc * 512:(tc + 1) * 512], bka[0:16, :]),
                    waits=[("s_gsm", gsm)])
                P.emit("vector", lambda e, tc=tc, bkb=bkb: e.tensor_copy(
                    gbb[:, tc * 512:(tc + 1) * 512], bkb[0:16, :]),
                    inc=("s_gse", 1))
            elif APART == 6.9:
                P.emit("vector", lambda e, tc=tc, bka=bka: e.tensor_copy(
                    gab[:, tc * 512:(tc + 1) * 512], bka[0:16, :]),
                    waits=[("s_gsm", gsm)])
                P.emit("vector", lambda e, tc=tc, bkb=bkb: e.tensor_copy(
                    gbb[:, tc * 512:(tc + 1) * 512], bkb[0:16, :]),
                    inc=("s_gsx", 1))
                gsx = P.val("s_gsx")
                P.emit("scalar", lambda e, tc=tc: e.activation(
                    gab[:, tc * 512:(tc + 1) * 512],
                    gab[:, tc * 512:(tc + 1) * 512], AF.Sigmoid),
                    waits=[("s_gsx", gsx)])
                P.emit("scalar", lambda e, tc=tc: e.activation(
                    gbb[:, tc * 512:(tc + 1) * 512],
                    gbb[:, tc * 512:(tc + 1) * 512], AF.Sigmoid),
                    inc=("s_gse", 1))
            else:
                P.emit("scalar", lambda e, tc=tc, bka=bka: e.activation(
                    gab[:, tc * 512:(tc + 1) * 512], bka[0:16, :], AF.Sigmoid),
                    waits=[("s_gsm", gsm)])
                P.emit("scalar", lambda e, tc=tc, bkb=bkb: e.activation(
                    gbb[:, tc * 512:(tc + 1) * 512], bkb[0:16, :], AF.Sigmoid),
                    inc=("s_gse", 1))
        gse = P.val("s_gse")
        if APART < 7:
            return
        P.emit("vector", lambda e: e.tensor_scalar(
            g8[:, :], g8[:, :], gconst[:, :], 1.0, ALU.mult, ALU.subtract) if False else e.tensor_scalar(
            g8[:, :], gbb[:, :], gconst[:, :], 1.0, ALU.mult, ALU.subtract),
            waits=[("s_gse", gse), ("s_small", smallv),
                   ("s_smalls", P.val("s_smalls"))])
        P.emit("vector", lambda e: e.tensor_mul(g8[:, :], gab[:, :], g8[:, :]),
               drain=True)
        P.emit("vector", lambda e: e.tensor_scalar(
            g8[:, :], g8[:, :], 8.0, 16.0, ALU.mult, ALU.add), inc=("s_g8", 1),
            drain=True)
        P.emit("vector", lambda e: e.tensor_reduce(
            bnd[:, 1:2], g8[:, :], AX.X, ALU.max, apply_absolute_value=True),
            inc=("s_bnd", 1), drain=True)

    def phaseM():
        if MPART < 0:
            return
        for pi, (src, col) in enumerate(((qT, 2), (kT, 3))):
            for j in range(8):
                P.emit("scalar", lambda e, j=j, src=src: e.activation(
                    sqt[:, :], src[:, j * NT:(j + 1) * NT], AF.Square),
                    waits=[("s_sqc", P.val("s_sqc"))], inc=("s_sq", 1))
                sq = P.val("s_sq")
                if MPART < 1:
                    continue
                for tc in range(4):
                    w = [("s_sq", sq)] if tc == 0 else []
                    if j == 0:
                        w = w + [("s_qne", P.val("s_qne"))]
                    P.emit("tensor", lambda e, j=j, tc=tc: e.matmul(
                        pb[tc][0:16, :], maskblk[:, 16 * j:16 * (j + 1)],
                        sqt[:, tc * 512:(tc + 1) * 512],
                        start=(j == 0), stop=(j == 7)),
                        waits=w,
                        inc=("s_sqc", 1) if tc == 3 else None)
            if MPART < 1:
                continue
            for tc in range(4):
                P.emit("vector", lambda e, tc=tc: e.tensor_copy(
                    qn[:, tc * 512:(tc + 1) * 512], pb[tc][0:16, :]),
                    waits=[("s_sqc", P.val("s_sqc"))],
                    inc=("s_qne", 1))
            P.emit("vector", lambda e, col=col: e.tensor_reduce(
                bnd[:, col:col + 1], qn[:, :], AX.X, ALU.max),
                waits=[("s_qne", P.val("s_qne"))], inc=("s_bnd", 1))
        if MPART < 2:
            return
        bv = P.val("s_bnd")
        P.emit("scalar", lambda e: e.activation(
            bnd[:, 2:3], bnd[:, 2:3], AF.Sqrt), waits=[("s_bnd", bv)])
        P.emit("scalar", lambda e: e.activation(
            bnd[:, 3:4], bnd[:, 3:4], AF.Sqrt), inc=("s_bsq", 1))
        P.emit("vector", lambda e: e.tensor_mul(
            bnd[:, 4:5], bnd[:, 2:3], bnd[:, 3:4]),
            waits=[("s_bsq", P.val("s_bsq"))])
        P.emit("vector", lambda e: e.tensor_mul(
            bnd[:, 5:6], bnd[:, 0:1], bnd[:, 1:2]))
        P.emit("vector", lambda e: e.tensor_scalar(
            bnd[:, 4:5], bnd[:, 4:5], 0.125, None, ALU.mult), drain=True)
        P.emit("vector", lambda e: e.tensor_scalar(
            bnd[:, 5:6], bnd[:, 5:6], 0.125, None, ALU.mult))
        P.emit("vector", lambda e: e.tensor_add(
            bnd[:, 6:7], bnd[:, 4:5], bnd[:, 5:6]), drain=True)
        P.emit("vector", lambda e: e.tensor_scalar(
            bnd[:, 6:7], bnd[:, 6:7], -1.0, -1.0, ALU.mult, ALU.add),
            inc=("s_negm", 1), drain=True)
        if MPART < 3:
            return
        P.emit("sync", lambda e: e.dma_start(negmrow[0:1, :], bnd[:, 6:7]),
               waits=[("s_negm", P.val("s_negm"))], inc=("s_negd", 16))
        if MPART < 4:
            return
        P.emit("tensor", lambda e: e.matmul(
            pb[4][:, 0:16], ones_f32[0:1, 0:128], negmrow[0:1, :],
            start=True, stop=True),
            waits=[("s_negd", P.val("s_negd"))], inc=("s_negb", 1))
        P.emit("vector", lambda e: e.tensor_copy(mbcast[:, :], pb[4][:, 0:16]),
               waits=[("s_negb", P.val("s_negb"))], inc=("s_mb", 1))

    def phaseB():
        mbv = P.val("s_mb")
        P.emit("sync", lambda e: e.dma_start(
            rwin[0][:, :], sap(d_rbd, 0, [[1, 128], [1, 1], [1, RWINW]])),
            waits=[("s_rbd", P.val("s_rbd"))], inc=("s_rw0", 16))
        P.emit("sync", lambda e: e.dma_start(gcur[0][0:1, :], g8[0:1, :]),
               waits=[("s_g8", P.val("s_g8"))], inc=("s_gc0", 16))

        tmp_head_end = {}
        qki3 = {}
        exp_par = {0: 0, 1: 0}
        pvm3 = {}
        aev_pv = {0: 0, 1: 0}
        rec_pv = {0: 0, 1: 0}
        gb_half = {}
        rw_val = {0: P.val("s_rw0")}
        tmp_pre = {}

        def emit_tmp(hh, qb, kt):
            hb3 = hh % 2
            tb = kt % 3
            u0 = 2048 + 128 * kt - qb
            w = [("s_gbe", gb_half[(hh, 0 if qb == 0 else 1)]),
                 ("s_rw" + str(hb3), rw_val[hh])]
            if qki3.get(tb):
                w.append(("s_qki", qki3[tb]))
            P.emit("vector", lambda e, hb=hb3, tb=tb, qb=qb, u0=u0:
                   e.tensor_mul(
                       tmpb[tb][:, :],
                       gb8c[hb][:, qb:qb + 1024],
                       sap(rwin[hb], u0, [[RWINW, 128], [-1, 1024]])),
                   waits=w, inc=("s_tmp", 1))
            tmp_pre[(hh, qb, kt)] = P.val("s_tmp")

        def emit_gbcast(h2):
            hb2 = h2 % 2
            gcv2 = P.val("s_gc" + str(hb2))
            w0 = [("s_gc" + str(hb2), gcv2),
                  ("s_exp", P.val("s_exp")),
                  ("s_qki", P.val("s_qki"))]
            for c in range(4):
                bank = psQK[c // 2][:, (c % 2) * 512:(c % 2) * 512 + 512]
                P.emit("tensor", lambda e, c=c, hb2=hb2, bank=bank: e.matmul(
                    bank, ones_bf[0:1, 0:128],
                    gcur[hb2][0:1, c * 512:(c + 1) * 512],
                    start=True, stop=True),
                    waits=w0 if c == 0 else [], inc=("s_gbm", 1))
            P.emit("vector", lambda e, hb2=hb2: e.tensor_copy(
                gb8c[hb2][:, 0:1024], PS[0][:, :]),
                waits=[("s_gbm", P.val("s_gbm"))], inc=("s_gbe", 1))
            gb_half[(h2, 0)] = P.val("s_gbe")
            emit_tmp(h2, 0, 0)
            P.emit("vector", lambda e, hb2=hb2: e.tensor_copy(
                gb8c[hb2][:, 1024:2048], PS[1][:, :]), inc=("s_gbe", 1))
            gb_half[(h2, 1)] = P.val("s_gbe")
            emit_tmp(h2, 0, 1)

        for h in range(NH):
            hb = h % 2
            chunk = h // 2
            prow = (h % 2) * 64
            kb = chunk * NT
            if h + 1 < NH:
                nb = (h + 1) % 2
                w = [("s_tmp", tmp_head_end.get(h - 1, 0))]
                P.emit("sync", lambda e, h=h, nb=nb: e.dma_start(
                    rwin[nb][:, :],
                    sap(d_rbd, (h + 1) * RBW, [[1, 128], [1, 1], [1, RWINW]])),
                    waits=w, inc=("s_rw" + str(nb), 16))
                P.emit("sync", lambda e, h=h, nb=nb: e.dma_start(
                    gcur[nb][0:1, :], g8[h + 1:h + 2, :]),
                    waits=w, inc=("s_gc" + str(nb), 16))
                rw_val[h + 1] = P.val("s_rw" + str(nb))
            if h == 0:
                emit_gbcast(0)

            for pr_ in range(2):
                qbase = pr_ * 1024
                pvt = psPV[pr_]
                exp_vals = {}
                def emit_pv(kt):
                    w = [("s_exp", exp_vals[kt])]
                    if kt == 0 and aev_pv[pr_]:
                        w += [("s_aev", aev_pv[pr_]), ("s_rec", rec_pv[pr_])]
                    for qc in range(2):
                        P.emit("tensor", lambda e, h=h, kt=kt, qc=qc,
                               pvt=pvt: e.matmul(
                                   pvt[0:65, qc * 512:qc * 512 + 512],
                                   v_aug[:, (kt * NH + h) * VGS: (kt * NH + h) * VGS + 65],
                                   attT[kt % 3][:, qc * 512:qc * 512 + 512],
                                   start=(kt == 0), stop=(kt == 15)),
                               waits=w if qc == 0 else [],
                               inc=("s_pvm", 1) if qc == 1 else None)
                    pvm3[kt % 3] = P.val("s_pvm")
                for kt in range(16):
                    par = kt % 2
                    tb = kt % 3
                    qkt = psQK[par]
                    if (h, qbase, kt) not in tmp_pre:
                        emit_tmp(h, qbase, kt)
                    tmpv = tmp_pre[(h, qbase, kt)]
                    w = []
                    if exp_par[par]:
                        w.append(("s_exp", exp_par[par]))
                    if kt < 2:
                        w.append(("s_gbe", gb_half[(h, kt)]))
                    for qc in range(2):
                        P.emit("tensor", lambda e, kb=kb, pr2=prow, kt=kt,
                               qbase=qbase, qc=qc, qkt=qkt: e.matmul(
                                   qkt[:, qc * 512:qc * 512 + 512],
                                   kT[pr2:pr2 + 64, kb + kt * 128: kb + kt * 128 + 128],
                                   qT[pr2:pr2 + 64,
                                      kb + qbase + qc * 512: kb + qbase + qc * 512 + 512],
                                   start=True, stop=False),
                               waits=w if qc == 0 else [])
                    w = [("s_tmp", tmpv)]
                    for qc in range(2):
                        P.emit("tensor", lambda e, kt=kt, qc=qc, tb=tb,
                               qkt=qkt: e.matmul(
                                   qkt[:, qc * 512:qc * 512 + 512], ident[:, :],
                                   tmpb[tb][:, qc * 512:qc * 512 + 512],
                                   start=False, stop=True),
                               waits=w if qc == 0 else [],
                               inc=("s_qki", 1) if qc == 1 else None)
                    qki3[tb] = P.val("s_qki")
                    w = [("s_qki", P.val("s_qki")), ("s_mb", mbv)]
                    if pvm3.get(tb):
                        w.append(("s_pvm", pvm3[tb]))
                    P.emit("scalar", lambda e, h=h, tb=tb, qkt=qkt:
                           e.activation(attT[tb][:, :], qkt[:, :], AF.Exp,
                                        bias=mbcast[:, h:h + 1], scale=0.125),
                           waits=w, inc=("s_exp", 1))
                    exp_par[par] = P.val("s_exp")
                    exp_vals[kt] = P.val("s_exp")
                    if kt >= 1:
                        emit_pv(kt - 1)
                emit_pv(15)
                if pr_ == 0:
                    emit_tmp(h, 1024, 0)
                    emit_tmp(h, 1024, 1)
                elif h + 1 < NH:
                    emit_gbcast(h + 1)
                P.emit("vector", lambda e, pvt=pvt: e.reciprocal(
                    rec[64:65, :], pvt[64:65, :]),
                    waits=[("s_pvm", P.val("s_pvm")),
                           ("s_rcd", P.val("s_rcd"))], inc=("s_rec", 1))
                rec_pv[pr_] = P.val("s_rec")
                P.emit("sync", lambda e, h=h, qbase=qbase: e.dma_start(
                    recall[h:h + 1, qbase:qbase + 1024], rec[64:65, :]),
                    waits=[("s_rec", P.val("s_rec"))], inc=("s_rcd", 16))
                if h % 2 == 0:
                    dst_fn = lambda e, chunk=chunk, qbase=qbase: attnoutT[
                        0:64, chunk * NT + qbase: chunk * NT + qbase + 1024]
                else:
                    dst_fn = lambda e, qbase=qbase: stage[0:64, qbase:qbase + 1024]
                w = [("s_pvm", P.val("s_pvm"))]
                if h % 2 == 1:
                    w.append(("s_stg", P.val("s_stg")))
                P.emit("vector", lambda e, pvt=pvt, dst_fn=dst_fn: e.tensor_copy(
                    dst_fn(e), pvt[0:64, :]), waits=w, inc=("s_aev", 1))
                aev_pv[pr_] = P.val("s_aev")
            tmp_head_end[h] = P.val("s_tmp")
            if h % 2 == 1:
                P.emit("sync", lambda e, chunk=chunk: e.dma_start(
                    attnoutT[64:128, chunk * NT:(chunk + 1) * NT],
                    stage[0:64, :]),
                    waits=[("s_aev", P.val("s_aev"))], inc=("s_stg", 16))

    def phaseC():
        aev = P.val("s_aev")
        stg = P.val("s_stg")
        rcd = P.val("s_rcd")
        for h in range(NH):
            chunk, prow = h // 2, (h % 2) * 64
            P.emit("sync", lambda e, h=h: e.dma_start(
                rrow[0:1, :], recall[h:h + 1, :]),
                waits=([("s_rcd", rcd), ("s_aev", aev), ("s_stg", stg)]
                       if h == 0 else [("s_anm", P.val("s_anm"))]),
                inc=("s_rrd", 16))
            rrv = P.val("s_rrd")
            for qc in range(4):
                bank = pb[4 + qc % 2] if prow == 0 else pb[6 + qc % 2]
                bank_out = bass.AP(bank.tensor, bank.offset + prow * 1024,
                                   [[1024, 64], [1, 1], [1, 512]])
                P.emit("tensor", lambda e, qc=qc, bank_out=bank_out:
                       e.matmul(bank_out,
                                ones_f32[0:1, 0:64],
                                rrow[0:1, qc * 512:(qc + 1) * 512],
                                start=True, stop=True),
                       waits=[("s_rrd", rrv),
                              ("s_anm", max(0, P.val("s_anm") - 4))],
                       inc=("s_abm", 1))
                P.emit("vector", lambda e, h=h, qc=qc, chunk=chunk, prow=prow,
                       bank_out=bank_out: e.scalar_tensor_tensor(
                           attnoutT[prow:prow + 64,
                                    chunk * NT + qc * 512: chunk * NT + qc * 512 + 512],
                           bank_out, 1.0,
                           attnoutT[prow:prow + 64,
                                    chunk * NT + qc * 512: chunk * NT + qc * 512 + 512],
                           ALU.mult, ALU.mult),
                       waits=[("s_abm", P.val("s_abm"))], inc=("s_anm", 1))
        anm = P.val("s_anm")
        if DEBUG:
            P.emit("sync", lambda e: e.dma_start(d_dbg_ao[:, :], attnoutT[:, :]),
                   waits=[("s_aev", aev), ("s_stg", stg)], inc=("s_dbg", 16))
            P.emit("sync", lambda e: e.dma_start(d_dbg_mb[:, :], mbcast[:, :]),
                   waits=[("s_mb", P.val("s_mb"))], inc=("s_dbg", 16))
            P.emit("gpsimd", lambda e: e.dma_start(d_dbg_g8[:, :], g8[:, :]),
                   waits=[("s_g8", P.val("s_g8"))], inc=("s_dbg2", 16))
        for oc in range(2):
            P.emit("gpsimd", lambda e, oc=oc: e.dma_start(
                sap(woSt[oc], 0, [[4096, 128], [512, 8], [1, 512]]),
                sap(d_wo, oc * 512, [[EMB, 128], [128 * EMB, 8], [1, 512]])),
                inc=("s_wo", 16))
        wov = P.val("s_wo")
        spill = []
        for t in range(TT):
            b = t % 2
            w = [("s_cd", spill[t - 2][0])] if t >= 2 else []
            P.emit("gpsimd", lambda e, t=t, b=b: e.dma_start(
                xtokC[b][:, :], d_x[t * 128:(t + 1) * 128, :]),
                waits=w, inc=("s_xin2", 16))
            xv = P.val("s_xin2")
            for oc in range(2):
                bank = pb[(t * 2 + oc) % 4]
                for ec in range(8):
                    w = []
                    if ec == 0:
                        w = [("s_afe", max(0, P.val("s_afe") - 2))]
                        if t == 0 and oc == 0:
                            w += [("s_anm", anm), ("s_wo", wov)]
                    P.emit("tensor", lambda e, t=t, oc=oc, ec=ec, bank=bank:
                           e.matmul(
                               bank[:, :],
                               attnoutT[:, ec * NT + t * 128: ec * NT + t * 128 + 128],
                               woSt[oc][:, ec * 512:(ec + 1) * 512],
                               start=(ec == 0), stop=False), waits=w)
                P.emit("tensor", lambda e, oc=oc, bank=bank: e.matmul(
                    bank[:, :], ones_bf[0:1, 0:128],
                    biases[0:1, 3072 + oc * 512: 3072 + oc * 512 + 512],
                    start=False, stop=True), inc=("s_wom", 1))
                P.emit("vector", lambda e, b=b, oc=oc, bank=bank: e.tensor_copy(
                    af[b][:, oc * 512:(oc + 1) * 512], bank[:, :]),
                    waits=[("s_wom", P.val("s_wom"))], inc=("s_afe", 1))
            afv = P.val("s_afe")
            sv = ln_stats(lambda e, b=b: af[b][:, :], 2, 3,
                          wait_v=[("s_afe", afv)], wait_s=[("s_afe", afv)])
            P.emit("vector", lambda e, b=b: e.tensor_scalar(
                af[b][:, :], af[b][:, :], stats[:, 2:3], stats[:, 3:4],
                ALU.subtract, ALU.mult), waits=[("s_lnst", sv)],
                inc=("s_lnu", 1))
            w = [("s_xin2", xv)]
            if t >= 2:
                w.append(("s_sp" + str(b), spill[t - 2][1]))
            P.emit("vector", lambda e, b=b: e.tensor_add(
                x1t[b][:, :], af[b][:, :], xtokC[b][:, :]),
                waits=w, inc=("s_x1", 1), drain=True)
            x1v = P.val("s_x1")
            spv = P.emit("gpsimd", lambda e, t=t, b=b: e.dma_start(
                d_x1[t * 128:(t + 1) * 128, :], x1t[b][:, :]),
                waits=[("s_x1", x1v)], inc=("s_sp" + str(b), 16))
            sv = ln_stats(lambda e, b=b: x1t[b][:, :], 4, 5,
                          wait_v=[("s_x1", x1v)], wait_s=[("s_x1", x1v)])
            wN = [("s_lnst", sv)]
            if t >= 2:
                wN.append(("s_cd", spill[t - 2][0]))
            P.emit("vector", lambda e, b=b: e.tensor_scalar(
                xlnC[b][:, :], x1t[b][:, :], stats[:, 4:5], stats[:, 5:6],
                ALU.subtract, ALU.mult), waits=wN,
                inc=[("s_x2ln", 1), ("s_lnu", 1)])
            x2v = P.val("s_x2ln")
            for c in range(8):
                P.emit("sync", lambda e, t=t, b=b, c=c: e.dma_start_transpose(
                    x2lnT[:, c * NT + t * 128: c * NT + (t + 1) * 128],
                    xlnC[b][:, c * 128:(c + 1) * 128]),
                    waits=[("s_x2ln", x2v)] if c == 0 else [],
                    inc=("s_cd", 16))
            spill.append((P.val("s_cd"), spv))
        P.emit("gpsimd", None, waits=[("s_sp0", P.val("s_sp0")),
                                      ("s_sp1", P.val("s_sp1"))])
        P.emit("sync", None, waits=[("s_cd", P.val("s_cd"))])

    def phaseD():
        cdv = P.val("s_cd")
        smallv = P.val("s_small")
        def emit_router(t):
            bank = pb[t % 2]
            for ec in range(8):
                w = []
                if ec == 0:
                    w = [("s_rte", P.val("s_rte")),
                         ("s_rt2", P.val("s_rt2")),
                         ("s_cd", min(cdv, 128 * (t + 1))),
                         ("s_acm", P.val("s_acm"))]
                    if t == 0:
                        w += [("s_small", smallv),
                              ("s_smalls", P.val("s_smalls")),
                              ("s_afe", P.val("s_afe"))]
                P.emit("tensor", lambda e, t=t, ec=ec, bank=bank: e.matmul(
                    bank[:, 0:9],
                    x2lnT[:, ec * NT + t * 128: ec * NT + t * 128 + 128],
                    gw9[:, ec * 9:(ec + 1) * 9],
                    start=(ec == 0), stop=(ec == 7)), waits=w,
                    inc=("s_rtm", 1) if ec == 7 else None)
            rtv = P.val("s_rtm")
            P.emit("vector", lambda e, bank=bank: e.tensor_copy(
                rcalc[:, 0:8], bank[:, 0:8]),
                waits=[("s_rtm", rtv), ("s_rtx", P.val("s_rtx"))])
            P.emit("vector", lambda e: e.max(out=rcalc[:, 8:16],
                                             in_=rcalc[:, 0:8]), drain=True)
            P.emit("vector", lambda e: e.tensor_scalar_mul(
                rcalc[:, 16:17], rcalc[:, 8:9], -1.0), inc=("s_rtn", 1),
                drain=True)
            P.emit("scalar", lambda e: e.activation(
                rcalc[:, 17:25], rcalc[:, 0:8], AF.Exp,
                bias=rcalc[:, 16:17], accum_out=rcalc[:, 25:26]),
                waits=[("s_rtn", P.val("s_rtn"))], inc=("s_rtx", 1))
            P.emit("vector", lambda e: e.reciprocal(
                rcalc[:, 26:27], rcalc[:, 25:26]),
                waits=[("s_rtx", P.val("s_rtx"))])
            P.emit("vector", lambda e: e.tensor_single_scalar(
                rcalc[:, 0:8], rcalc[:, 0:8], rcalc[:, 9:10], ALU.is_ge),
                drain=True)
            P.emit("vector", lambda e: e.tensor_mul(
                rcalc[:, 0:8], rcalc[:, 0:8], rcalc[:, 17:25]), drain=True)
            P.emit("vector", lambda e, t=t: e.tensor_scalar_mul(
                wtok[:, t * 8:(t + 1) * 8], rcalc[:, 0:8], rcalc[:, 26:27]),
                inc=("s_rte", 1), drain=True)
            P.emit("scalar", lambda e, t=t, bank=bank: e.activation(
                swtok[:, t:t + 1], bank[:, 8:9], AF.Sigmoid),
                waits=[("s_rtm", rtv)], inc=("s_rt2", 1))
        rte_half = {}

        def gu_ap(pe, i, gu):
            if pe < 8:
                col = i * 128 + gu * MI
                return sap(d_egu, pe * EMB * 2 * MI + col,
                           [[2 * MI, 128], [128 * 2 * MI, 8], [1, 128]])
            src = d_sg if gu == 0 else d_su
            col = (pe - 8) * MI + i * 128
            return sap(src, col, [[SI, 128], [128 * SI, 8], [1, 128]])

        def wd_ap(pe):
            if pe < 8:
                return sap(d_ed, pe * MI * EMB,
                           [[EMB, 128], [128 * EMB, ICH], [1, EMB]])
            return sap(d_sd, (pe - 8) * MI * EMB,
                       [[EMB, 128], [128 * EMB, ICH], [1, EMB]])

        wd_cons = [0, 0]
        wg_cons = [0, 0]
        eadd_half = 0
        out_par = {0: [], 1: []}
        for half in range(2):
            rte = rt2 = None
            P.emit("vector", lambda e: e.memset(acc[:, :], 0.0),
                   waits=[("s_eadd", eadd_half),
                          ("s_anm", P.val("s_anm"))], inc=("s_accz", 1))
            accz = P.val("s_accz")
            for pe in range(10):
                wdb = pe % 2
                w = [("s_dnm", wd_cons[wdb])]
                if half == 0 and pe == 0:
                    w.append(("s_wom", P.val("s_wom")))
                P.emit("gpsimd", lambda e, pe=pe, wdb=wdb: e.dma_start(
                    sap(wd[wdb], 0, [[ICH * 1024, 128], [1024, ICH], [1, 1024]]),
                    wd_ap(pe)), waits=w,
                    inc=("s_wd" + str(wdb), 16))
                wdv = P.val("s_wd" + str(wdb))
                for i in range(ICH):
                    wgb = i % 2
                    P.emit("gpsimd", lambda e, pe=pe, i=i, wgb=wgb: e.dma_start(
                        sap(wg[wgb], 0, [[1024, 128], [128, 8], [1, 128]]),
                        gu_ap(pe, i, 0)),
                        waits=[("s_gum", wg_cons[wgb])],
                        inc=("s_wg" + str(wgb), 16))
                    P.emit("gpsimd", lambda e, pe=pe, i=i, wgb=wgb: e.dma_start(
                        sap(wu[wgb], 0, [[1024, 128], [128, 8], [1, 128]]),
                        gu_ap(pe, i, 1)),
                        inc=("s_wg" + str(wgb), 16))
                    wgv = P.val("s_wg" + str(wgb))
                    for tc in range(2):
                        par = (i * 2 + tc) % 2
                        bg, bu_ = pb[4 + 2 * par], pb[5 + 2 * par]
                        for ec in range(8):
                            w = []
                            if ec == 0:
                                w = [("s_wg" + str(wgb), wgv),
                                     ("s_hev", max(0, P.val("s_hev") - 1))]
                                if half == 0 and pe == 0 and i == 0 and tc == 0:
                                    w.append(("s_abm", P.val("s_abm")))
                            P.emit("tensor", lambda e, i=i, tc=tc, ec=ec,
                                   bg=bg, half=half, wgb=wgb: e.matmul(
                                       bg[:, :],
                                       wg[wgb][:, ec * 128:(ec + 1) * 128],
                                       x2lnT[:, ec * NT + half * 1024 + tc * 512:
                                             ec * NT + half * 1024 + tc * 512 + 512],
                                       start=(ec == 0), stop=(ec == 7)),
                                   waits=w,
                                   inc=("s_ggm", 1) if ec == 7 else None)
                        for ec in range(8):
                            P.emit("tensor", lambda e, i=i, tc=tc, ec=ec,
                                   bu_=bu_, half=half, wgb=wgb: e.matmul(
                                       bu_[:, :],
                                       wu[wgb][:, ec * 128:(ec + 1) * 128],
                                       x2lnT[:, ec * NT + half * 1024 + tc * 512:
                                             ec * NT + half * 1024 + tc * 512 + 512],
                                       start=(ec == 0), stop=(ec == 7)),
                                   inc=("s_gum", 1) if ec == 7 else None)
                        wg_cons[wgb] = P.val("s_gum")
                        ggm, gum = P.val("s_ggm"), P.val("s_gum")
                        P.emit("scalar", lambda e, par=par, bg=bg: e.activation(
                            gsil[par][:, :], bg[:, :], AF.Silu),
                            waits=[("s_ggm", ggm),
                                   ("s_hev", max(0, P.val("s_hev") - 1))],
                            inc=("s_sil", 1))
                        P.emit("vector", lambda e, i=i, tc=tc, par=par, bu_=bu_:
                               e.scalar_tensor_tensor(
                                   hbuf[:, i * 1024 + tc * 512: i * 1024 + tc * 512 + 512],
                                   bu_[:, :], 1.0, gsil[par][:, :],
                                   ALU.mult, ALU.mult),
                               waits=[("s_gum", gum),
                                      ("s_sil", P.val("s_sil")),
                                      ("s_dnm", P.val("s_dnm"))],
                               inc=("s_hev", 1))
                    if pe == 0 and i < 8:
                        emit_router(half * 8 + i)
                if pe == 0:
                    rte_half[half] = (P.val("s_rte"), P.val("s_rt2"))
                    rte, rt2 = rte_half[half]
                hev = P.val("s_hev")
                if DEBUG and half == 0 and pe == 0:
                    P.emit("sync", lambda e: e.dma_start(d_dbg_h[:, :], hbuf[:, :]),
                           waits=[("s_hev", hev)], inc=("s_dbg", 16))
                for tt in range(8):
                    for oc in range(2):
                        bank = pb[(tt * 2 + oc) % 4]
                        for i in range(ICH):
                            w = []
                            if i == 0:
                                w = [("s_acm", max(0, P.val("s_acm") - 2))]
                                if tt == 0 and oc == 0:
                                    w += [("s_hev", hev),
                                          ("s_wd" + str(wdb), wdv),
                                          ("s_accz", accz)]
                                    if pe == 0:
                                        w += [("s_rte", rte), ("s_rt2", rt2)]
                            P.emit("tensor", lambda e, tt=tt, oc=oc, i=i,
                                   bank=bank, wdb=wdb: e.matmul(
                                       bank[:, :],
                                       hbuf[:, i * 1024 + tt * 128: i * 1024 + tt * 128 + 128],
                                       wd[wdb][:, i * 1024 + oc * 512: i * 1024 + oc * 512 + 512],
                                       start=(i == 0), stop=(i == ICH - 1)),
                                   waits=w,
                                   inc=("s_dnm", 1) if i == ICH - 1 else None)
                        dnm = P.val("s_dnm")
                        if pe < 8:
                            scal = lambda e, half=half, tt=tt, pe=pe: wtok[
                                :, (half * 8 + tt) * 8 + pe:
                                   (half * 8 + tt) * 8 + pe + 1]
                        else:
                            scal = lambda e, half=half, tt=tt: swtok[
                                :, half * 8 + tt: half * 8 + tt + 1]
                        P.emit("vector", lambda e, tt=tt, oc=oc, bank=bank,
                               scal=scal: e.scalar_tensor_tensor(
                                   acc[:, tt * 1024 + oc * 512:
                                       tt * 1024 + oc * 512 + 512],
                                   bank[:, :], scal(e),
                                   acc[:, tt * 1024 + oc * 512:
                                       tt * 1024 + oc * 512 + 512],
                                   ALU.mult, ALU.add),
                               waits=[("s_dnm", dnm), ("s_rte", rte),
                                      ("s_rt2", rt2)],
                               inc=("s_acm", 1))
                wd_cons[wdb] = P.val("s_dnm")
            acm = P.val("s_acm")
            if DEBUG and half == 0:
                P.emit("sync", lambda e: e.dma_start(d_dbg_acc[:, :], acc[:, :]),
                       waits=[("s_acm", acm)], inc=("s_dbg", 16))
            for tt in range(8):
                t = half * 8 + tt
                b = tt % 2
                w = [("s_eadd", max(0, P.val("s_eadd") - 1))]
                if half == 0 and tt == 0:
                    w += [("s_sp0", P.val("s_sp0")), ("s_sp1", P.val("s_sp1"))]
                P.emit("gpsimd", lambda e, t=t, b=b: e.dma_start(
                    x1r[b][:, :], d_x1[t * 128:(t + 1) * 128, :]),
                    waits=w, inc=("s_x1r", 16))
                xrv = P.val("s_x1r")
                sv = ln_stats(lambda e, tt=tt: acc[:, tt * 1024:(tt + 1) * 1024],
                              6, 7, wait_v=[("s_acm", acm)],
                              wait_s=[("s_acm", acm)])
                w = [("s_lnst", sv)]
                if len(out_par[b]) >= 1:
                    w.append(("s_ou" + str(b), out_par[b][-1]))
                P.emit("vector", lambda e, tt=tt, b=b: e.tensor_scalar(
                    outt[b][:, :], acc[:, tt * 1024:(tt + 1) * 1024],
                    stats[:, 6:7], stats[:, 7:8], ALU.subtract, ALU.mult),
                    waits=w, inc=("s_lnu", 1))
                P.emit("vector", lambda e, b=b: e.tensor_add(
                    outt[b][:, :], outt[b][:, :], x1r[b][:, :]),
                    waits=[("s_x1r", xrv)], inc=("s_eadd", 1), drain=True)
                ov = P.emit("sync", lambda e, t=t, b=b: e.dma_start(
                    d_out[t * 128:(t + 1) * 128, :], outt[b][:, :]),
                    waits=[("s_eadd", P.val("s_eadd"))],
                    inc=("s_ou" + str(b), 16))
                out_par[b].append(ov)
            eadd_half = P.val("s_eadd")

        if DEBUG:
            P.emit("sync", lambda e: e.dma_start(d_dbg_w[:, :], wtok[:, :]),
                   waits=[("s_rte", P.val("s_rte"))], inc=("s_dbg", 16))
            P.emit("sync", lambda e: e.dma_start(d_dbg_sw[:, :], swtok[:, :]),
                   waits=[("s_rt2", P.val("s_rt2"))], inc=("s_dbg", 16))
            P.emit("sync", lambda e: e.dma_start(d_dbg_x2[:, :], x2lnT[:, :]),
                   waits=[("s_cd", cdv)], inc=("s_dbg", 16))
        fw = [("s_ou0", P.val("s_ou0")), ("s_ou1", P.val("s_ou1"))]
        if DEBUG:
            fw.append(("s_dbg", P.val("s_dbg")))
            fw.append(("s_dbg2", P.val("s_dbg2")))
        P.emit("sync", None, waits=fw)

    phase0()
    if upto >= 1:
        phaseA()
        P.barrier()
    if upto >= 2:
        phaseM()
        P.barrier()
    if upto >= 3:
        phaseB()
        P.barrier()
    if upto >= 4:
        phaseC()
    if upto >= 5:
        phaseD()

    from contextlib import ExitStack
    with ExitStack() as ctx:
        hands = {}
        for s in list(P.semn):
            hands[s] = ctx.enter_context(nc.semaphore(s))
        block = ctx.enter_context(nc.Block())

        def runner(name):
            def run(eng):
                for fn, waits, incs, drain in P.ops[name]:
                    if drain:
                        eng.drain()
                    for s, v in waits:
                        eng.wait_ge(hands[s], v)
                    if fn is not None:
                        ins = fn(eng)
                        if incs:
                            ins.then_inc(hands[incs[0][0]], incs[0][1])
                        for s, k in incs[1:]:
                            eng.nop().then_inc(hands[s], k)
            return run

        block.tensor(runner("tensor"))
        block.vector(runner("vector"))
        block.scalar(runner("scalar"))
        block.sync(runner("sync"))
        block.gpsimd(runner("gpsimd"))
    return nc


def kernel(
    L0_attn_hidden_states, L0_attn_Wq, L0_attn_bq, L0_attn_Wk, L0_attn_bk,
    L0_attn_Wv, L0_attn_bv, L0_attn_Wo, L0_attn_bo, L0_attn_gru_const,
    L0_attn_gru_W, L0_attn_gru_b, L0_attn_rel_embed, L0_ffn_gate_w,
    L0_ffn_experts_gate_up, L0_ffn_experts_down, L0_ffn_shared_gate_w,
    L0_ffn_shared_up_w, L0_ffn_shared_down_w, L0_ffn_shared_expert_gate_w,
):
    f32 = np.float32
    X = np.ascontiguousarray(np.asarray(L0_attn_hidden_states, f32))
    bcat = np.zeros((1, 4352), f32)
    bcat[0, 0:1024] = np.asarray(L0_attn_bq, f32)
    bcat[0, 1024:2048] = np.asarray(L0_attn_bk, f32)
    bcat[0, 2048:3072] = np.asarray(L0_attn_bv, f32)
    bcat[0, 3072:4096] = np.asarray(L0_attn_bo, f32)
    bcat[0, 4096:4224] = np.tile(np.asarray(L0_attn_gru_b, f32), 16)
    gruW = np.asarray(L0_attn_gru_W, f32)
    gwb = np.zeros((EMB, 128), f32)
    for h in range(NH):
        gwb[h * 64:(h + 1) * 64, h * 8:(h + 1) * 8] = gruW.T
    gs = np.zeros((128, 32), f32)
    for h in range(NH):
        for c in range(4):
            gs[h * 8 + c, h] = 1.0
            gs[h * 8 + 4 + c, 16 + h] = 1.0
    shared = dict(
        wqT=np.ascontiguousarray(np.asarray(L0_attn_Wq, f32).T),
        wkT=np.ascontiguousarray(np.asarray(L0_attn_Wk, f32).T),
        wvT=np.ascontiguousarray(np.asarray(L0_attn_Wv, f32).T),
        woT=np.ascontiguousarray(np.asarray(L0_attn_Wo, f32).T),
        bcat=bcat, gruWblk=gwb, gruS=gs,
        gruconst=np.ascontiguousarray(
            np.asarray(L0_attn_gru_const, f32).reshape(16, 1)),
        ident=np.eye(128, dtype=f32),
        relemb=np.ascontiguousarray(np.asarray(L0_attn_rel_embed, f32)),
        oh=_host_onehot(),
        gateWseg=np.ascontiguousarray(np.concatenate(
            [np.asarray(L0_ffn_gate_w, f32).T,
             np.asarray(L0_ffn_shared_expert_gate_w, f32).T], axis=1)),
        eguT=np.ascontiguousarray(
            np.asarray(L0_ffn_experts_gate_up, f32).transpose(0, 2, 1)),
        edT=np.ascontiguousarray(
            np.asarray(L0_ffn_experts_down, f32).transpose(0, 2, 1)),
        sgT=np.ascontiguousarray(np.asarray(L0_ffn_shared_gate_w, f32).T),
        suT=np.ascontiguousarray(np.asarray(L0_ffn_shared_up_w, f32).T),
        sdT=np.ascontiguousarray(np.asarray(L0_ffn_shared_down_w, f32).T),
    )
    in_maps = [dict(shared, x=np.ascontiguousarray(X[b])) for b in range(8)]
    nc = build_nc()
    res = run_bass_kernel_spmd(nc, in_maps, core_ids=list(range(8)))
    return np.stack([np.asarray(r["out"], f32) for r in res.results], axis=0)
